# version 01 (final)
# baseline (speedup 1.0000x reference)
import numpy as np
import ml_dtypes
import concourse.bass as bass
import concourse.mybir as mybir
from concourse.bass_utils import run_bass_kernel_spmd
from contextlib import ExitStack

F32 = mybir.dt.float32
BF16 = mybir.dt.bfloat16
AF = mybir.ActivationFunctionType
ALU = mybir.AluOpType
AX = mybir.AxisListType.X

B, L, HID = 2, 4096, 4096
H, P, N, G = 128, 64, 128, 8
INTER = H * P
CONV = INTER + 2 * G * N
PROJ = INTER + CONV + H
NCORES = 8
TMAIN = 1024
W = 256
HALO = 3
NTOK = HALO + W + TMAIN
CS = 128
NCH = (W + TMAIN) // CS
NWARM = W // CS
KCH = HID // 128
FFN_INT, LAT, NE = 8192, 2048, 8
EPS = 1e-5
NEG = -1e30
NOC_XBC = CONV // 128
OC_DT = NOC_XBC
NOC_G = INTER // 128


class Sem:
    def __init__(self, h, name=""):
        self.h = h
        self.n = 0
        self.name = name

    def inc(self, inst, k=1):
        inst.then_inc(self.h, k)
        self.n += k
        return self.n

    def dinc(self, inst):
        return self.inc(inst, 16)

    def wait(self, eng, n=None):
        eng.wait_ge(self.h, self.n if n is None else n)


def tok_splits(total, maxf=512, base=0):
    out, o = [], 0
    while o < total:
        f = min(maxf, total - o)
        out.append((base + o, f))
        o += f
    return out


def build(dbg_stage=0):
    nc = bass.Bass("TRN2", target_bir_lowering=False,
                   detect_race_conditions=False)

    xin = nc.dram_tensor("xin", [HID, NTOK], F32, kind="ExternalInput").ap()
    w_projT = nc.dram_tensor("w_projT", [HID, PROJ], F32, kind="ExternalInput").ap()
    w_outT = nc.dram_tensor("w_outT", [INTER, HID], F32, kind="ExternalInput").ap()
    w_up = nc.dram_tensor("w_up", [HID, FFN_INT], F32, kind="ExternalInput").ap()
    w_down = nc.dram_tensor("w_down", [FFN_INT, HID], F32, kind="ExternalInput").ap()
    w_fc1 = nc.dram_tensor("w_fc1", [HID, LAT], F32, kind="ExternalInput").ap()
    w_exp = nc.dram_tensor("w_exp", [LAT, LAT], F32, kind="ExternalInput").ap()
    w_fc2 = nc.dram_tensor("w_fc2", [LAT, HID], F32, kind="ExternalInput").ap()
    w_gate = nc.dram_tensor("w_gate", [HID, NE], F32, kind="ExternalInput").ap()
    conv_wT = nc.dram_tensor("conv_wT", [128, (CONV // 128) * 4], F32, kind="ExternalInput").ap()
    conv_bT = nc.dram_tensor("conv_bT", [128, CONV // 128], F32, kind="ExternalInput").ap()
    dt_bias = nc.dram_tensor("dt_bias", [128, 1], F32, kind="ExternalInput").ap()
    a_log = nc.dram_tensor("a_log", [128, 1], F32, kind="ExternalInput").ap()
    d_exp = nc.dram_tensor("d_exp", [1, INTER], F32, kind="ExternalInput").ap()
    nw_exp = nc.dram_tensor("nw_exp", [1, INTER], F32, kind="ExternalInput").ap()
    normw_T = nc.dram_tensor("normw_T", [128, INTER // 128], F32, kind="ExternalInput").ap()
    gam_T = nc.dram_tensor("gam_T", [128, HID // 128], F32, kind="ExternalInput").ap()
    bet_T = nc.dram_tensor("bet_T", [128, HID // 128], F32, kind="ExternalInput").ap()
    m0T = nc.dram_tensor("m0T", [128, 128], BF16, kind="ExternalInput").ap()
    idb = nc.dram_tensor("idb", [128, 128], BF16, kind="ExternalInput").ap()
    idf = nc.dram_tensor("idf", [128, 128], F32, kind="ExternalInput").ap()
    warm_mask = nc.dram_tensor("warm_mask", [128, 1], F32, kind="ExternalInput").ap()

    out = nc.dram_tensor("out", [HID, TMAIN], F32, kind="ExternalOutput").ap()

    def scratch(name, shape, dt, dbg=False):
        kind = "ExternalOutput" if (dbg and dbg_stage) else "Internal"
        return nc.dram_tensor(name, shape, dt, kind=kind).ap()

    sp_xbc = scratch("sp_xbc", [CONV, NTOK], BF16, dbg=(dbg_stage == 1))
    sp_dt = scratch("sp_dt", [128, NTOK], F32, dbg=(dbg_stage == 1))
    sp_sg = scratch("sp_sg", [INTER, TMAIN], BF16, dbg=(dbg_stage == 1))
    sp_xn = scratch("sp_xn", [HID, NTOK], BF16, dbg=(dbg_stage in (1, 9)))
    sp_post = scratch("sp_post", [CONV, W + TMAIN], BF16, dbg=(dbg_stage == 2))
    fl_cum = scratch("fl_cum", [NCH, 3, CS * 128], BF16)
    fl_ecum = scratch("fl_ecum", [NCH, CS * 128], BF16)
    fl_cdec = scratch("fl_cdec", [NCH, 128], F32)
    fl_rstd = scratch("fl_rstd", [NCH, G * CS], F32)
    fl_r = scratch("fl_r", [2, 512], F32)
    fl_g = scratch("fl_g", [16, 512], BF16)
    sp_gn = scratch("sp_gn", [TMAIN, INTER], BF16, dbg=(dbg_stage in (3, 31, 34, 36, 40, 41, 42, 43)))
    sp_attn = scratch("sp_attn", [HID, TMAIN], F32, dbg=(dbg_stage == 4))
    sp_pre = scratch("sp_pre", [HID, TMAIN], BF16, dbg=(dbg_stage == 5))
    sp_s = scratch("sp_s", [128, INTER], F32, dbg=(dbg_stage in (3, 31, 33, 36, 40, 41)))
    sp_e = scratch("sp_e", [8 * 128, 16384], BF16, dbg=(dbg_stage in (35, 40)))
    sp_y = scratch("sp_y", [8 * 128, INTER], BF16, dbg=(dbg_stage == 41))
    sp_gp = scratch("sp_gp", [8 * 128, INTER], BF16, dbg=(dbg_stage in (42, 43)))
    sp_ms = scratch("sp_ms", [8 * 128, 16], F32, dbg=(dbg_stage == 43))
    sp_rsb = scratch("sp_rsb", [8 * 128, 8], BF16, dbg=(dbg_stage == 43))
    sp_xw = scratch("sp_xw", [8 * 128, INTER], BF16, dbg=(dbg_stage == 43))
    sp_xg = scratch("sp_xg", [8 * 128, INTER], BF16, dbg=(dbg_stage == 42))
    sp_sgt = scratch("sp_sgt", [8 * 128, INTER], BF16, dbg=(dbg_stage == 42))
    sp_mlc = scratch("sp_mlc", [NCH * 128, 128], F32, dbg=(dbg_stage in (35, 40)))

    with ExitStack() as ctx:
        e = ctx.enter_context
        sLD = Sem(e(nc.semaphore("sLD")))
        sGD = Sem(e(nc.semaphore("sGD")))
        sXL = Sem(e(nc.semaphore("sXL")))
        sSP = Sem(e(nc.semaphore("sSP")))
        sPE = Sem(e(nc.semaphore("sPE")))
        sDV = Sem(e(nc.semaphore("sDV")))
        sSC = Sem(e(nc.semaphore("sSC")))
        sST = Sem(e(nc.semaphore("sST")))
        sAT = Sem(e(nc.semaphore("sAT")))
        sW = [Sem(e(nc.semaphore(f"sW{i}"))) for i in range(2)]
        sF8 = [Sem(e(nc.semaphore(f"sF8_{i}"))) for i in range(2)]
        sEC = [Sem(e(nc.semaphore(f"sEC{i}"))) for i in range(2)]
        sRB = Sem(e(nc.semaphore("sRB")))
        sEH = Sem(e(nc.semaphore("sEH")))
        sSE = Sem(e(nc.semaphore("sSE")))
        sYH = Sem(e(nc.semaphore("sYH")))
        sSTM = Sem(e(nc.semaphore("sSTM")))
        sSN = Sem(e(nc.semaphore("sSN")))
        sGP = Sem(e(nc.semaphore("sGP")))
        sC2 = Sem(e(nc.semaphore("sC2")))

        block = e(nc.Block())

        def barrier():
            sems = ([sLD, sGD, sXL, sSP, sPE, sDV, sSC, sRB, sEH, sSE,
                     sYH, sSTM, sSN, sGP, sC2] + sW + sF8 + sEC)

            @block.tensor
            def _(t):
                for s in sems:
                    s.wait(t)

            @block.vector
            def _(v):
                for s in sems:
                    s.wait(v)

            @block.scalar
            def _(s_):
                for s in sems:
                    s.wait(s_)

            @block.gpsimd
            def _(g):
                for s in sems:
                    s.wait(g)

            @block.sync
            def _(sy):
                for s in sems:
                    s.wait(sy)

        convw_t = e(nc.sbuf_tensor("convw_t", [128, (CONV // 128) * 4], BF16))
        convb_t = e(nc.sbuf_tensor("convb_t", [128, CONV // 128], BF16))
        dtb_t = e(nc.sbuf_tensor("dtb_t", [128, 1], F32))
        acoef_t = e(nc.sbuf_tensor("acoef_t", [128, 1], F32))
        normw_t = e(nc.sbuf_tensor("normw_t", [128, INTER // 128], BF16))
        gam_t = e(nc.sbuf_tensor("gam_t", [128, HID // 128], F32))
        bet_t = e(nc.sbuf_tensor("bet_t", [128, HID // 128], F32))
        m0T_t = e(nc.sbuf_tensor("m0T_t", [128, 128], BF16))
        idb_t = e(nc.sbuf_tensor("idb_t", [128, 128], BF16))
        idf_t = e(nc.sbuf_tensor("idf_t", [128, 128], F32))
        wmask_t = e(nc.sbuf_tensor("wmask_t", [128, 1], F32))
        onesb = e(nc.sbuf_tensor("onesb", [1, 128], BF16))
        onescb = e(nc.sbuf_tensor("onescb", [128, 1], BF16))


        CONSTS = [EPS, -0.5, -1.0, 1.0, 1.0 / 1024.0]
        cvals = e(nc.sbuf_tensor("cvals", [128, len(CONSTS)], F32))
        for ci, cv in enumerate(CONSTS):
            nc.const_aps.register(F32, cv, cvals[:, ci:ci + 1])

        @block.vector
        def _(v):
            for ci, cv in enumerate(CONSTS):
                i = v.memset(cvals[:, ci:ci + 1], cv)
            sDV.inc(i)
        dv_consts = sDV.n

        xn_ctx = ExitStack()
        xn = xn_ctx.enter_context(nc.sbuf_tensor("xn", [128, KCH * NTOK], BF16))

        def xn_k(k, o=0, f=NTOK):
            return xn[:, k * NTOK + o:k * NTOK + o + f]

        with nc.sbuf_tensor("x2s", [128, 2 * NTOK], BF16) as x2s, \
             nc.sbuf_tensor("alog_s", [128, 1], F32) as alog_s, \
             nc.sbuf_tensor("mu_b", [128, NTOK], BF16) as mu_b, \
             nc.sbuf_tensor("rs_b", [128, NTOK], BF16) as rs_b, \
             nc.sbuf_tensor("rowf", [1, 4 * NTOK], F32) as rowf, \
             nc.sbuf_tensor("rowb", [1, 2 * NTOK], BF16) as rowb:

            @block.gpsimd
            def _(g):
                for dst, src in [(convw_t, conv_wT), (convb_t, conv_bT),
                                 (dtb_t, dt_bias), (alog_s, a_log), (gam_t, gam_T),
                                 (bet_t, bet_T), (m0T_t, m0T), (idb_t, idb),
                                 (idf_t, idf), (wmask_t, warm_mask)]:
                    sGD.dinc(g.dma_start(out=dst[:], in_=src))
                for k in range(KCH):
                    sXL.dinc(g.dma_start(out=xn_k(k),
                                         in_=xin[k * 128:(k + 1) * 128, :]))

            @block.vector
            def _(v):
                v.memset(onesb[:], 1.0)
                sDV.inc(v.memset(onescb[:], 1.0))

            @block.scalar
            def _(s):
                sGD.wait(s)
                sSC.inc(s.activation(acoef_t[:], alog_s[:], AF.Exp,
                                     bias=0.0, scale=1.0))

            with nc.psum_tensor("ps_sx", [1, NTOK], F32) as ps_sx, \
                 nc.psum_tensor("ps_sxx", [1, NTOK], F32) as ps_sxx:

                sc_x2_base = sSC.n
                pe_st_base = sPE.n

                @block.scalar
                def _(s):
                    sXL.wait(s)
                    for k in range(KCH):
                        if k >= 2:
                            s.wait_ge(sPE.h, pe_st_base + k - 1)
                        sSC.inc(s.activation(
                            x2s[:, (k % 2) * NTOK:(k % 2) * NTOK + NTOK],
                            xn_k(k), AF.Square, bias=0.0, scale=1.0))

                @block.tensor
                def _(t):
                    sDV.wait(t)
                    sXL.wait(t)
                    for k in range(KCH):
                        t.wait_ge(sSC.h, sc_x2_base + k + 1)
                        for (o, f) in tok_splits(NTOK):
                            t.matmul(ps_sx[:, o:o + f], onescb[:], xn_k(k, o, f),
                                     start=(k == 0), stop=(k == KCH - 1))
                            i = t.matmul(ps_sxx[:, o:o + f],
                                         onescb[:], x2s[:, (k % 2) * NTOK + o:
                                                        (k % 2) * NTOK + o + f],
                                         start=(k == 0), stop=(k == KCH - 1))
                        sPE.inc(i)

                @block.vector
                def _(v):
                    sPE.wait(v)
                    mu = rowf[:, 0:NTOK]
                    m2 = rowf[:, NTOK:2 * NTOK]
                    var = rowf[:, 2 * NTOK:3 * NTOK]
                    v.tensor_scalar(mu, ps_sx[:], 1.0 / HID, None, ALU.mult)
                    v.tensor_scalar(m2, ps_sxx[:], 1.0 / HID, None, ALU.mult)
                    v.tensor_tensor(var, mu, mu, ALU.mult)
                    sDV.inc(v.tensor_tensor(var, m2, var, ALU.subtract))

                @block.scalar
                def _(s):
                    sDV.wait(s)
                    var = rowf[:, 2 * NTOK:3 * NTOK]
                    rstd = rowf[:, 3 * NTOK:4 * NTOK]
                    s.activation(rstd, var, AF.Ln, bias=EPS, scale=1.0)
                    sSC.inc(s.nop())
                    s.wait_ge(sSC.h, sSC.n)
                    s.activation(rstd, rstd, AF.Exp, bias=0.0, scale=-0.5)
                    s.copy(rowb[:, 0:NTOK], rowf[:, 0:NTOK])
                    sSC.inc(s.nop())
                    s.wait_ge(sSC.h, sSC.n)
                    sSC.inc(s.copy(rowb[:, NTOK:2 * NTOK], rstd))

                barrier()

            with nc.psum_tensor("ps_bc1", [128, NTOK], F32) as ps_bc1, \
                 nc.psum_tensor("ps_bc2", [128, NTOK], F32) as ps_bc2:

                @block.tensor
                def _(t):
                    for (o, f) in tok_splits(NTOK):
                        t.matmul(ps_bc1[:, o:o + f], onesb[:], rowb[:, o:o + f],
                                 start=True, stop=True)
                        i = t.matmul(ps_bc2[:, o:o + f], onesb[:],
                                     rowb[:, NTOK + o:NTOK + o + f],
                                     start=True, stop=True)
                    sPE.inc(i)

                @block.scalar
                def _(s):
                    sPE.wait(s)
                    s.copy(mu_b[:], ps_bc1[:])
                    sSC.inc(s.copy(rs_b[:], ps_bc2[:]))

                @block.vector
                def _(v):
                    sSC.wait(v)
                    for k in range(KCH):
                        v.tensor_tensor(xn_k(k), xn_k(k), mu_b[:], ALU.subtract)
                        i = v.tensor_tensor(xn_k(k), xn_k(k), rs_b[:], ALU.mult)
                    sDV.inc(i)

                @block.sync
                def _(sy):
                    sDV.wait(sy)
                    for k in range(KCH):
                        sSP.dinc(sy.dma_start(out=sp_xn[k * 128:(k + 1) * 128, :],
                                              in_=xn_k(k)))

                barrier()

        if dbg_stage == 9:
            @block.gpsimd
            def _(g):
                for s in [sSP, sLD, sGD]:
                    s.wait(g)
            return nc

        with nc.sbuf_tensor("wt", [128, 2 * KCH * 128], BF16) as wt, \
             nc.sbuf_tensor("ebb", [128, 2 * NTOK], BF16) as ebb, \
             nc.sbuf_tensor("ebf", [128, NTOK], F32) as ebf, \
             nc.psum_tensor("ps_p1a", [128, 2 * 1536], F32) as ps_p1a:

            NOC = NOC_XBC + 1 + NOC_G
            pe_base = sPE.n
            gd_base = sGD.n
            dv_base = sDV.n
            sc_base = sSC.n
            sp_base = sSP.n

            def oc_cols(oc):
                if oc < NOC_XBC:
                    return INTER + oc * 128, tok_splits(NTOK)
                if oc == OC_DT:
                    return INTER + CONV + 0, tok_splits(NTOK)
                return (oc - (NOC_XBC + 1)) * 128, tok_splits(TMAIN, base=HALO + W)

            @block.gpsimd
            def _(g):
                for oc in range(NOC):
                    col, _sp = oc_cols(oc)
                    if oc >= 2:
                        g.wait_ge(sPE.h, pe_base + oc - 1)
                    src = w_projT[:, col:col + 128].rearrange(
                        "(kc kp) o -> kp kc o", kp=128)
                    dst = wt[:, (oc % 2) * KCH * 128:((oc % 2) + 1) * KCH * 128] \
                        .rearrange("p (kc o) -> p kc o", o=128)
                    sW[oc % 2].dinc(g.dma_start(out=dst, in_=src))

            @block.tensor
            def _(t):
                w_base = [sW[0].n - 0, sW[1].n - 0]
                w_base = [sW[i].n - 16 * ((NOC + 1 - i) // 2) for i in range(2)]
                for oc in range(NOC):
                    _col, sp = oc_cols(oc)
                    t.wait_ge(sW[oc % 2].h, w_base[oc % 2] + 16 * (oc // 2 + 1))
                    if oc >= 2:
                        t.wait_ge(sDV.h if (oc - 2) <= OC_DT else sSC.h,
                                  (dv_base + (oc - 2) + 1) if (oc - 2) <= OC_DT
                                  else (sc_base + (oc - 2) - OC_DT))
                    pso = (oc % 2) * 1536
                    for (o, f) in sp:
                        po = o if oc <= OC_DT else o - (HALO + W)
                        for k in range(KCH):
                            i = t.matmul(ps_p1a[:, pso + po:pso + po + f],
                                         wt[:, (oc % 2) * KCH * 128 + k * 128:
                                            (oc % 2) * KCH * 128 + (k + 1) * 128],
                                         xn_k(k, o, f),
                                         start=(k == 0), stop=(k == KCH - 1))
                    sPE.inc(i)

            @block.vector
            def _(v):
                for oc in range(NOC_XBC + 1):
                    v.wait_ge(sPE.h, pe_base + oc + 1)
                    if oc >= 2:
                        v.wait_ge(sSP.h, sp_base + 16 * oc)
                    pso = (oc % 2) * 1536
                    dst = ebf[:] if oc == OC_DT else \
                        ebb[:, (oc % 2) * NTOK:(oc % 2) * NTOK + NTOK]
                    sDV.inc(v.tensor_copy(dst, ps_p1a[:, pso:pso + NTOK]))

            @block.scalar
            def _(s):
                for oc in range(NOC_XBC + 1, NOC):
                    s.wait_ge(sPE.h, pe_base + oc + 1)
                    s.wait_ge(sSP.h, sp_base + 16 * oc)
                    pso = (oc % 2) * 1536
                    sSC.inc(s.activation(
                        ebb[:, (oc % 2) * NTOK:(oc % 2) * NTOK + TMAIN],
                        ps_p1a[:, pso:pso + TMAIN], AF.Silu,
                        bias=0.0, scale=1.0))

            @block.sync
            def _(sy):
                for oc in range(NOC):
                    if oc <= OC_DT:
                        sy.wait_ge(sDV.h, dv_base + oc + 1)
                    else:
                        sy.wait_ge(sSC.h, sc_base + (oc - OC_DT))
                    if oc < NOC_XBC:
                        dst = sp_xbc[oc * 128:(oc + 1) * 128, :]
                        src = ebb[:, (oc % 2) * NTOK:(oc % 2) * NTOK + NTOK]
                    elif oc == OC_DT:
                        dst = sp_dt
                        src = ebf[:]
                    else:
                        go = oc - (NOC_XBC + 1)
                        dst = sp_sg[go * 128:(go + 1) * 128, :]
                        src = ebb[:, (oc % 2) * NTOK:(oc % 2) * NTOK + TMAIN]
                    sSP.dinc(sy.dma_start(out=dst, in_=src))

            barrier()

        xn_ctx.close()

        if dbg_stage == 1:
            @block.gpsimd
            def _(g):
                for s in [sSP, sLD, sGD]:
                    s.wait(g)
            return nc

        NBLK = CONV // 128
        with nc.sbuf_tensor("xc", [128, NBLK * 131], BF16) as xc, \
             nc.sbuf_tensor("pb", [128, NBLK * 128], BF16) as pb, \
             nc.sbuf_tensor("cu", [128, NBLK * 128], BF16) as cu, \
             nc.sbuf_tensor("sout", [128, 2 * NBLK * 128], BF16) as sout:

            xc3 = xc[:].rearrange("p (c t) -> p c t", t=131)
            pb3 = pb[:].rearrange("p (c t) -> p c t", t=128)
            cu3 = cu[:].rearrange("p (c t) -> p c t", t=128)
            cw4 = convw_t[:].rearrange("p (c j) -> p c j", j=4)

            ld0, dv0, sc0, sp0 = sLD.n, sDV.n, sSC.n, sSP.n

            @block.sync
            def _(sy):
                for cc in range(NCH):
                    if cc >= 1:
                        sy.wait_ge(sDV.h, dv0 + cc * 8)
                    src = sp_xbc[:, cc * 128:cc * 128 + 131].rearrange(
                        "(c p) t -> p c t", p=128)
                    sLD.dinc(sy.dma_start(out=xc3, in_=src))
                    if cc >= 1:
                        sy.wait_ge(sSC.h, sc0 + cc)
                        po = ((cc - 1) % 2) * NBLK * 128
                        sSP.dinc(sy.dma_start(
                            out=sp_post[:, (cc - 1) * 128:cc * 128].rearrange(
                                "(c p) t -> p c t", p=128),
                            in_=sout[:, po:po + NBLK * 128].rearrange(
                                "p (c t) -> p c t", t=128)))
                sy.wait_ge(sSC.h, sc0 + NCH)
                po = ((NCH - 1) % 2) * NBLK * 128
                sSP.dinc(sy.dma_start(
                    out=sp_post[:, (NCH - 1) * 128:NCH * 128].rearrange(
                        "(c p) t -> p c t", p=128),
                    in_=sout[:, po:po + NBLK * 128].rearrange(
                        "p (c t) -> p c t", t=128)))

            @block.vector
            def _(v):
                for cc in range(NCH):
                    v.wait_ge(sLD.h, ld0 + 16 * (cc + 1))
                    if cc >= 2:
                        v.wait_ge(sSC.h, sc0 + cc - 1)
                    for j in range(4):
                        wj = cw4[:, :, j:j + 1].to_broadcast([128, NBLK, 128])
                        dst = pb3 if j == 0 else cu3
                        v.tensor_tensor(dst, xc3[:, :, j:j + 128], wj, ALU.mult)
                        if j > 0:
                            v.tensor_tensor(pb3, pb3, cu3, ALU.add)
                    i = v.tensor_tensor(
                        pb3, pb3,
                        convb_t[:].unsqueeze(2).to_broadcast([128, NBLK, 128]),
                        ALU.add)
                    sDV.inc(i, 8)

            @block.scalar
            def _(s):
                for cc in range(NCH):
                    s.wait_ge(sDV.h, dv0 + (cc + 1) * 8)
                    if cc >= 2:
                        s.wait_ge(sSP.h, sp0 + 16 * (cc - 1))
                    po = (cc % 2) * NBLK * 128
                    sSC.inc(s.activation(sout[:, po:po + NBLK * 128], pb[:],
                                         AF.Silu, bias=0.0, scale=1.0))

            barrier()

        if dbg_stage == 2:
            @block.gpsimd
            def _(g):
                for s in [sSP, sLD, sGD, sSC, sDV]:
                    s.wait(g)
            return nc

        env31 = locals() | {"dbg_gp": dbg_stage in (31, 34),
                            "dbg33": dbg_stage == 33,
                            "dbg34": dbg_stage == 34,
                            "dbg35": dbg_stage == 35,
                            "dbg36": dbg_stage == 36,
                            "dbg40": dbg_stage == 40,
                            "dbg41": dbg_stage == 41,
                            "dbg42": dbg_stage == 42,
                            "dbg43": dbg_stage == 43}
        build_ssd(nc, block, env31)
        if dbg_stage in (3, 31, 33, 34, 35, 36, 40, 41, 42, 43):
            @block.gpsimd
            def _(g):
                for s in [sSP, sLD, sGD, sSC, sDV, sPE]:
                    s.wait(g)
            return nc

        build_rest(nc, block, locals())

    return nc


def build_ssd(nc, block, env):
    g_ = env
    sLD, sGD, sSP, sPE, sDV, sSC = (g_["sLD"], g_["sGD"], g_["sSP"],
                                    g_["sPE"], g_["sDV"], g_["sSC"])
    sF8, sEC = g_["sF8"], g_["sEC"]
    sRB, sEH, sSE, sYH = g_["sRB"], g_["sEH"], g_["sSE"], g_["sYH"]
    sSTM, sSN, sGP, sC2 = g_["sSTM"], g_["sSN"], g_["sGP"], g_["sC2"]
    sp_post, sp_sg, sp_dt, sp_gn, sp_s = (g_["sp_post"], g_["sp_sg"],
                                          g_["sp_dt"], g_["sp_gn"], g_["sp_s"])
    fl_cum, fl_ecum, fl_cdec = g_["fl_cum"], g_["fl_ecum"], g_["fl_cdec"]
    dtb_t, acoef_t, m0T_t, idb_t, idf_t = (g_["dtb_t"], g_["acoef_t"],
                                           g_["m0T_t"], g_["idb_t"], g_["idf_t"])
    wmask_t, onesb = g_["wmask_t"], g_["onesb"]
    d_expA, nw_expA = g_["d_exp"], g_["nw_exp"]
    barrier = g_["barrier"]
    dbg35 = g_.get("dbg35", False)
    dbg36 = g_.get("dbg36", False)
    dbg40 = g_.get("dbg40", False)
    dbg41 = g_.get("dbg41", False)
    dbg42 = g_.get("dbg42", False)
    dbg43 = g_.get("dbg43", False)
    sp_e, sp_mlc, sp_y = g_.get("sp_e"), g_.get("sp_mlc"), g_.get("sp_y")
    sp_gp, sp_xg, sp_sgt = g_.get("sp_gp"), g_.get("sp_xg"), g_.get("sp_sgt")
    sp_ms, sp_rsb, sp_xw = g_.get("sp_ms"), g_.get("sp_rsb"), g_.get("sp_xw")
    ESL = 16 if dbg40 else (8 if (dbg35 or dbg36) else 2)

    def pbcast(ap2d, parts, n):
        return bass.AP(ap2d.tensor, ap2d.offset, [[0, parts], [1, n]])

    with ExitStack() as ctx:
        e = ctx.enter_context
        x_tok = e(nc.sbuf_tensor("x_tok", [128, INTER], BF16))
        sg_tok = e(nc.sbuf_tensor("sg_tok", [128, INTER], BF16))
        xsg = e(nc.sbuf_tensor("xsg", [128, INTER], BF16))
        bc_fm = e(nc.sbuf_tensor("bc_fm", [128, 2048], BF16))
        b_tok = e(nc.sbuf_tensor("b_tok", [128, 1024], BF16))
        gp = e(nc.sbuf_tensor("gp", [128, INTER], BF16))
        xw = e(nc.sbuf_tensor("xw", [128, INTER], BF16))
        s_st = e(nc.sbuf_tensor("s_st", [128, INTER], F32))
        s_bf = e(nc.sbuf_tensor("s_bf", [128, INTER], BF16))
        ecg = e(nc.sbuf_tensor("ecg", [128, 2 * 2048], BF16))
        c2 = e(nc.sbuf_tensor("c2", [128, 2 * 2048], BF16))
        scm = e(nc.sbuf_tensor("scm", [128, 1024], BF16))
        ebuf = e(nc.sbuf_tensor("ebuf", [128, ESL * 128], BF16))
        sebuf = e(nc.sbuf_tensor("sebuf", [128, 2 * 128], BF16))
        dtw = e(nc.sbuf_tensor("dtw", [128, 8 * 128], F32))
        wfm32 = e(nc.sbuf_tensor("wfm32", [128, 128], F32))
        lvls = e(nc.sbuf_tensor("lvls", [128, 3 * 128], BF16))
        ecum_b = e(nc.sbuf_tensor("ecum_b", [128, 128], BF16))
        cdec_f = e(nc.sbuf_tensor("cdec_f", [128, 1], F32))
        mlc_tok = e(nc.sbuf_tensor("mlc_tok", [128, 128], F32))
        w_tok = e(nc.sbuf_tensor("w_tok", [128, 128], BF16))
        cdec_bc = e(nc.sbuf_tensor("cdec_bc", [128, 128], F32))
        fcum3 = e(nc.sbuf_tensor("fcum3", [3, 2 * 1024], BF16))
        ones3 = e(nc.sbuf_tensor("ones3", [3, 128], BF16))
        rstd_t = e(nc.sbuf_tensor("rstd_t", [128, 2 * 8], F32))
        rsb = e(nc.sbuf_tensor("rsb", [128, 8], BF16))
        u5 = e(nc.sbuf_tensor("u5", [128, (2 if dbg41 else 1) * 512], BF16))
        d_bc = e(nc.sbuf_tensor("d_bc", [128, INTER], BF16))
        nw_bc = (d_bc if (dbg35 or dbg40)
                 else e(nc.sbuf_tensor("nw_bc", [128, INTER], BF16)))
        ps_R = e(nc.psum_tensor("ps_R", [128, 2048], F32))
        ps_y = e(nc.psum_tensor("ps_y", [128, 1024], F32))
        ps_st = e(nc.psum_tensor("ps_st", [128, 1024], F32))

        def dtt(i):
            return dtw[:, i * 128:(i + 1) * 128]
        Z, AZ, RZ, EZ, L1, DTV, CUM, MLC = range(8)
        LNDT, DA = EZ, L1

        @block.vector
        def _(v):
            v.memset(ones3[:], 1.0)
            sDV.inc(v.memset(s_st[:], 0.0))

        @block.gpsimd
        def _(g):
            sGD.dinc(g.dma_start(out=d_bc[:], in_=pbcast(d_expA, 128, INTER)))
            sGD.dinc(g.dma_start(out=nw_bc[:], in_=pbcast(nw_expA, 128, INTER)))

        for cc in range(NCH):
            main = cc >= NWARM
            t0 = cc * 128
            NB = H // 8

            @block.sync
            def _(sy):
                sLD.dinc(sy.dma_start(
                    out=x_tok[:], in_=sp_post[0:INTER, t0:t0 + 128],
                    transpose=True))
                sLD.dinc(sy.dma_start(
                    out=b_tok[:], in_=sp_post[INTER:INTER + 1024, t0:t0 + 128],
                    transpose=True))
                sLD.dinc(sy.dma_start(
                    out=bc_fm[:].rearrange("p (c t) -> p c t", t=128),
                    in_=sp_post[INTER:CONV, t0:t0 + 128].rearrange(
                        "(c p) t -> p c t", p=128)))
                if main:
                    sLD.dinc(sy.dma_start(
                        out=sg_tok[:], in_=sp_sg[:, t0 - W:t0 - W + 128],
                        transpose=True))
                sLD.dinc(sy.dma_start(
                    out=dtt(Z), in_=sp_dt[:, HALO + t0:HALO + t0 + 128]))

            @block.vector
            def _(v):
                sLD.wait(v)
                v.tensor_scalar(dtt(Z), dtt(Z), dtb_t[:, 0:1], None, ALU.add)
                v.tensor_scalar(dtt(AZ), dtt(Z), -1.0, None, ALU.mult)
                v.tensor_tensor(dtt(AZ), dtt(AZ), dtt(Z), ALU.max)
                i = v.tensor_scalar(dtt(RZ), dtt(Z), 0.0, None, ALU.max)
                sDV.inc(i)

            @block.scalar
            def _(s):
                sDV.wait(s)
                sSC.inc(s.activation(dtt(EZ), dtt(AZ), AF.Exp,
                                     bias=0.0, scale=-1.0))
                sSC.wait(s)
                sSC.inc(s.activation(dtt(L1), dtt(EZ), AF.Ln,
                                     bias=1.0, scale=1.0))

            @block.vector
            def _(v):
                sSC.wait(v)
                sDV.inc(v.tensor_tensor(dtt(DTV), dtt(L1), dtt(RZ), ALU.add))

            @block.scalar
            def _(s):
                sDV.wait(s)
                sSC.inc(s.activation(dtt(LNDT), dtt(DTV), AF.Ln,
                                     bias=0.0, scale=1.0))

            @block.vector
            def _(v):
                v.tensor_scalar(dtt(DA), dtt(DTV), acoef_t[:, 0:1], -1.0,
                                ALU.mult, ALU.mult)
                v.tensor_tensor_scan(dtt(CUM), dtt(DA), dtt(DA), 0.0,
                                     ALU.add, ALU.bypass)
                sSC.wait(v)
                v.tensor_tensor(dtt(MLC), dtt(LNDT), dtt(CUM), ALU.subtract)
                v.tensor_copy(lvls[:, 0:128], dtt(CUM))
                v.tensor_copy(dtt(AZ), lvls[:, 0:128])
                v.tensor_tensor(dtt(RZ), dtt(CUM), dtt(AZ), ALU.subtract)
                v.tensor_copy(lvls[:, 128:256], dtt(RZ))
                v.tensor_copy(dtt(AZ), lvls[:, 128:256])
                v.tensor_tensor(dtt(RZ), dtt(RZ), dtt(AZ), ALU.subtract)
                i = v.tensor_copy(lvls[:, 256:384], dtt(RZ))
                sDV.inc(i)

            @block.scalar
            def _(s):
                sDV.wait(s)
                s.activation(wfm32[:], dtt(MLC), AF.Exp,
                             bias=dtt(CUM)[:, 127:128], scale=1.0)
                s.activation(ecum_b[:], dtt(CUM), AF.Exp, bias=0.0, scale=1.0)
                i = s.activation(cdec_f[:], dtt(CUM)[:, 127:128], AF.Exp,
                                 bias=0.0, scale=1.0)
                sSC.inc(i)

            @block.tensor
            def _(t):
                sSC.wait(t)
                t.transpose(ps_st[:, 0:128], dtt(MLC), idf_t[:])
                i = t.transpose(ps_st[:, 128:256], wfm32[:], idf_t[:])
                sPE.inc(i)

            @block.vector
            def _(v):
                sPE.wait(v)
                v.tensor_copy(mlc_tok[:], ps_st[:, 0:128])
                i = v.tensor_copy(w_tok[:], ps_st[:, 128:256])
                i = v.tensor_tensor(
                    xw[:].rearrange("p (h q) -> p h q", q=64),
                    x_tok[:].rearrange("p (h q) -> p h q", q=64),
                    w_tok[:].unsqueeze(2).to_broadcast([128, 128, 64]),
                    ALU.mult)
                if main:
                    sGD.wait(v)
                    v.tensor_tensor(xsg[:], x_tok[:], sg_tok[:], ALU.mult)
                    i = v.tensor_tensor(xsg[:], xsg[:], d_bc[:], ALU.mult)
                    i = v.tensor_copy(s_bf[:], s_st[:])
                sDV.inc(i)

            @block.gpsimd
            def _(g):
                sDV.wait(g)
                sSC.wait(g)
                for lvl in range(3):
                    sGD.dinc(g.dma_start(
                        out=fl_cum[cc:cc + 1, lvl:lvl + 1].rearrange(
                            "o l (p t) -> (o l p) t", p=128),
                        in_=lvls[:, lvl * 128:(lvl + 1) * 128]))
                sGD.dinc(g.dma_start(
                    out=fl_ecum[cc:cc + 1, :].rearrange(
                        "o (p t) -> (o p) t", p=128),
                    in_=ecum_b[:]))
                sGD.dinc(g.dma_start(
                    out=fl_cdec[cc:cc + 1, :].rearrange("o p -> (o p)")
                    .unsqueeze(1),
                    in_=cdec_f[:]))
                g.wait_ge(sGD.h, sGD.n)
                sGD.dinc(g.dma_start(
                    out=cdec_bc[:], in_=pbcast(fl_cdec[cc:cc + 1, :], 128, 128)))

            if main:
                @block.tensor
                def _(t):
                    sLD.wait(t)
                    for g2 in range(G):
                        i = t.matmul(ps_y[:, g2 * 128:(g2 + 1) * 128],
                                     bc_fm[:, g2 * 128:(g2 + 1) * 128],
                                     bc_fm[:, 1024 + g2 * 128:
                                           1024 + (g2 + 1) * 128],
                                     start=True, stop=True)
                    sPE.inc(i)

                @block.vector
                def _(v):
                    sPE.wait(v)
                    sDV.inc(v.tensor_copy(scm[:], ps_y[:]))

            barrier()

            bF8 = [s.n for s in sF8]
            bEC = [s.n for s in sEC]
            bR, bE, bSE, bY = sRB.n, sEH.n, sSE.n, sYH.n
            bSTM, bSN, bG, bC2 = sSTM.n, sSN.n, sGP.n, sC2.n
            bSP = sSP.n

            if main and dbg42:
                bSP2 = sSP.n

                @block.sync
                def _(sy):
                    sy.wait_ge(sGP.h, bG + NB)
                    r0 = (cc - NWARM) * 128
                    sSP.dinc(sy.dma_start(
                        out=sp_gp[r0:r0 + 128, :], in_=gp[:]))
                    sSP.dinc(sy.dma_start(
                        out=sp_xg[r0:r0 + 128, :], in_=xsg[:]))
                    sSP.dinc(sy.dma_start(
                        out=sp_sgt[r0:r0 + 128, :], in_=sg_tok[:]))

            if main and dbg41:
                @block.sync
                def _(sy):
                    for bb in range(NB):
                        sy.wait_ge(sGP.h, bG + bb + 1)
                        sSP.dinc(sy.dma_start(
                            out=sp_y[(cc - NWARM) * 128:(cc - NWARM) * 128 + 128,
                                     bb * 512:(bb + 1) * 512],
                            in_=u5[:, (bb % 2) * 512:(bb % 2) * 512 + 512]))

            if main and (dbg35 or dbg40):
                @block.sync
                def _(sy):
                    sSP.dinc(sy.dma_start(
                        out=sp_mlc[cc * 128:(cc + 1) * 128, :], in_=mlc_tok[:]))
                    for bb in range(NB):
                        sy.wait_ge(sEH.h, bE + (bb + 1) * 8)
                        sSP.dinc(sy.dma_start(
                            out=sp_e[(cc - NWARM) * 128:(cc - NWARM) * 128 + 128,
                                     bb * 1024:(bb + 1) * 1024],
                            in_=ebuf[:, (bb % (ESL // 8)) * 1024:
                                     (bb % (ESL // 8)) * 1024 + 1024]))

            if main:
                @block.gpsimd
                def _(g):
                    for bb in range(NB):
                        if bb >= 2:
                            g.wait_ge(sRB.h, bR + bb - 1)
                        sF8[bb % 2].dinc(g.dma_start(
                            out=fcum3[0:3, (bb % 2) * 1024:(bb % 2) * 1024 + 1024],
                            in_=fl_cum[cc:cc + 1, :, bb * 1024:(bb + 1) * 1024]
                            .rearrange("o l t -> (o l) t")))
                        if bb % 2 == 0:
                            g2 = bb // 2
                            if g2 >= 2:
                                g.wait_ge(sC2.h, bC2 + g2 - 1)
                            sEC[g2 % 2].dinc(g.dma_start(
                                out=ecg[:, (g2 % 2) * 2048:(g2 % 2 + 1) * 2048],
                                in_=pbcast(fl_ecum[cc:cc + 1,
                                                   g2 * 2048:(g2 + 1) * 2048],
                                           128, 2048)))

                @block.scalar
                def _(s):
                    for h in range(H):
                        bb = h // 8
                        if h % 8 == 0:
                            s.wait_ge(sRB.h, bR + bb + 1)
                            if dbg35 and h >= 8:
                                s.wait_ge(sSP.h, bSP + 16 * (bb + 1))
                            if dbg40 and h >= 16:
                                s.wait_ge(sSP.h, bSP + 16 * bb)
                        if h >= 2:
                            s.wait_ge(sSE.h, bSE + h - 1)
                        rb = (bb % 2) * 1024
                        i = s.activation(
                            ebuf[:, (h % ESL) * 128:(h % ESL) * 128 + 128],
                            ps_R[:, rb + (h % 8) * 128:rb + (h % 8) * 128 + 128],
                            AF.Exp, bias=mlc_tok[:, h:h + 1], scale=1.0)
                        sEH.inc(i)

                @block.vector
                def _(v):
                    def gp_scan(bb):
                        v.wait_ge(sYH.h, bY + (bb + 1) * 8)
                        yb = bb % 2
                        if dbg41 and bb >= 2:
                            v.wait_ge(sSP.h, bSP + 16 * (bb - 1))
                        u5s = (u5[:, (bb % 2) * 512:(bb % 2) * 512 + 512]
                               if dbg41 else u5[:])
                        v.tensor_copy(u5s, ps_y[:, yb * 512:(yb + 1) * 512])
                        v.tensor_tensor(
                            gp[:, bb * 512:(bb + 1) * 512], u5s,
                            sg_tok[:, bb * 512:(bb + 1) * 512], ALU.mult)
                        i = v.tensor_tensor(
                            gp[:, bb * 512:(bb + 1) * 512],
                            gp[:, bb * 512:(bb + 1) * 512],
                            xsg[:, bb * 512:(bb + 1) * 512], ALU.add)
                        sGP.inc(i)
                        v.wait_ge(sSTM.h, bSTM + bb + 1)
                        v.tensor_tensor(
                            s_st[:, bb * 512:(bb + 1) * 512]
                            .rearrange("p (h q) -> p h q", q=64),
                            s_st[:, bb * 512:(bb + 1) * 512]
                            .rearrange("p (h q) -> p h q", q=64),
                            cdec_bc[:, bb * 8:(bb + 1) * 8]
                            .unsqueeze(2).to_broadcast([128, 8, 64]), ALU.mult)
                        i = v.tensor_tensor(
                            s_st[:, bb * 512:(bb + 1) * 512],
                            s_st[:, bb * 512:(bb + 1) * 512],
                            ps_st[:, (bb % 2) * 512:(bb % 2 + 1) * 512], ALU.add)
                        sSN.inc(i)

                    for bb in range(NB):
                        if bb % 2 == 0:
                            g2 = bb // 2
                            v.wait_ge(sEC[g2 % 2].h,
                                      bEC[g2 % 2] + 16 * (g2 // 2 + 1))
                            if g2 >= 2:
                                v.wait_ge(sYH.h, bY + (g2 - 1) * 16)
                            i = v.tensor_tensor(
                                c2[:, (g2 % 2) * 2048:(g2 % 2 + 1) * 2048]
                                .rearrange("p (h t) -> p h t", t=128),
                                bc_fm[:, 1024 + g2 * 128:1024 + (g2 + 1) * 128]
                                .unsqueeze(1).to_broadcast([128, 16, 128]),
                                ecg[:, (g2 % 2) * 2048:(g2 % 2 + 1) * 2048]
                                .rearrange("p (h t) -> p h t", t=128),
                                ALU.mult)
                            sC2.inc(i)
                        for hh in range(8):
                            h = bb * 8 + hh
                            v.wait_ge(sEH.h, bE + h + 1)
                            if h >= 2:
                                v.wait_ge(sYH.h, bY + h - 1)
                            g2 = h // 16
                            i = v.tensor_tensor(
                                sebuf[:, (h % 2) * 128:(h % 2) * 128 + 128],
                                ebuf[:, (h % ESL) * 128:(h % ESL) * 128 + 128],
                                scm[:, g2 * 128:(g2 + 1) * 128], ALU.mult)
                            sSE.inc(i)
                        if bb >= 1:
                            gp_scan(bb - 1)
                    gp_scan(NB - 1)

                @block.tensor
                def _(t):
                    def emit_R(bb):
                        t.wait_ge(sF8[bb % 2].h, bF8[bb % 2] + 16 * (bb // 2 + 1))
                        if bb >= 2:
                            t.wait_ge(sEH.h, bE + (bb - 1) * 8)
                        rb = (bb % 2) * 1024
                        for hh in range(8):
                            t.matmul(
                                ps_R[:, rb + hh * 128:rb + (hh + 1) * 128],
                                ones3[0:3, :],
                                fcum3[0:3, (bb % 2) * 1024 + hh * 128:
                                      (bb % 2) * 1024 + hh * 128 + 128],
                                start=(hh % 4 == 0), stop=False,
                                skip_group_check=True)
                        for hh in range(8):
                            i = t.matmul(
                                ps_R[:, rb + hh * 128:rb + (hh + 1) * 128],
                                m0T_t[:], idb_t[:], start=False,
                                stop=(hh % 4 == 3),
                                skip_group_check=True)
                        sRB.inc(i)

                    emit_R(0)
                    emit_R(1)
                    for bb in range(NB):
                        yb = bb % 2
                        for hh in range(8):
                            h = bb * 8 + hh
                            g2 = h // 16
                            t.wait_ge(sSE.h, bSE + h + 1)
                            if h % 16 == 0:
                                t.wait_ge(sC2.h, bC2 + g2 + 1)
                            if h >= 16 and h % 8 == 0:
                                t.wait_ge(sGP.h, bG + bb - 1)
                            ysl = ps_y[:, yb * 512 + hh * 64:yb * 512 + hh * 64 + 64]
                            t.matmul(ysl,
                                     sebuf[:, (h % 2) * 128:(h % 2) * 128 + 128],
                                     x_tok[:, h * 64:(h + 1) * 64],
                                     start=(hh == 0), stop=False,
                                     skip_group_check=True)
                            i = t.matmul(
                                ysl,
                                c2[:, (g2 % 2) * 2048 + (h % 16) * 128:
                                   (g2 % 2) * 2048 + (h % 16) * 128 + 128],
                                s_bf[:, h * 64:(h + 1) * 64],
                                start=False, stop=(hh == 7),
                                skip_group_check=True)
                            sYH.inc(i)
                        if bb >= 2:
                            t.wait_ge(sSN.h, bSN + bb - 1)
                        for h2 in range(bb * 8, bb * 8 + 8):
                            i = t.matmul(
                                ps_st[:, (bb % 2) * 512 + (h2 % 8) * 64:
                                      (bb % 2) * 512 + (h2 % 8) * 64 + 64],
                                b_tok[:, (h2 // 16) * 128:(h2 // 16) * 128 + 128],
                                xw[:, h2 * 64:(h2 + 1) * 64],
                                start=(h2 % 8 == 0), stop=(h2 % 8 == 7),
                                skip_group_check=True)
                        sSTM.inc(i)
                        if bb + 2 < NB:
                            emit_R(bb + 2)

            else:
                @block.tensor
                def _(t):
                    for bb in range(NB):
                        if bb >= 2:
                            t.wait_ge(sSN.h, bSN + bb - 1)
                        for h2 in range(bb * 8, bb * 8 + 8):
                            i = t.matmul(
                                ps_st[:, (bb % 2) * 512 + (h2 % 8) * 64:
                                      (bb % 2) * 512 + (h2 % 8) * 64 + 64],
                                b_tok[:, (h2 // 16) * 128:(h2 // 16) * 128 + 128],
                                xw[:, h2 * 64:(h2 + 1) * 64],
                                start=(h2 % 8 == 0), stop=(h2 % 8 == 7),
                                skip_group_check=True)
                        sSTM.inc(i)

                @block.vector
                def _(v):
                    for bb in range(NB):
                        v.wait_ge(sSTM.h, bSTM + bb + 1)
                        v.tensor_tensor(
                            s_st[:, bb * 512:(bb + 1) * 512]
                            .rearrange("p (h q) -> p h q", q=64),
                            s_st[:, bb * 512:(bb + 1) * 512]
                            .rearrange("p (h q) -> p h q", q=64),
                            cdec_bc[:, bb * 8:(bb + 1) * 8]
                            .unsqueeze(2).to_broadcast([128, 8, 64]), ALU.mult)
                        i = v.tensor_tensor(
                            s_st[:, bb * 512:(bb + 1) * 512],
                            s_st[:, bb * 512:(bb + 1) * 512],
                            ps_st[:, (bb % 2) * 512:(bb % 2 + 1) * 512], ALU.add)
                        sSN.inc(i)

                if cc == NWARM - 1:
                    @block.vector
                    def _(v):
                        v.wait_ge(sSN.h, sSN.n)
                        sDV.inc(v.tensor_scalar(s_st[:], s_st[:],
                                                wmask_t[:, 0:1], None, ALU.mult))

            barrier()

            if main:
                @block.scalar
                def _(s):
                    sSC.inc(s.activation(xw[:], gp[:], AF.Square,
                                         bias=0.0, scale=1.0))

                @block.vector
                def _(v):
                    sSC.wait(v)
                    rt = rstd_t[:, 0:8]
                    i = v.tensor_reduce(
                        rt.unsqueeze(2),
                        xw[:].rearrange("p (g q) -> p g q", q=1024),
                        AX, ALU.add)
                    sDV.inc(i)
                    sDV.wait(v)
                    sDV.inc(v.tensor_scalar(rt, rt, 1.0 / 1024.0, None,
                                            ALU.mult))
                    if dbg43:
                        sDV.inc(v.tensor_copy(rstd_t[:, 8:16], rt))

                @block.scalar
                def _(s):
                    sDV.wait(s)
                    rt = rstd_t[:, 0:8]
                    sSC.inc(s.activation(rt, rt, AF.Ln, bias=EPS, scale=1.0))
                    sSC.wait(s)
                    sSC.inc(s.activation(rt, rt, AF.Exp, bias=0.0, scale=-0.5))

                @block.vector
                def _(v):
                    sSC.wait(v)
                    if dbg42:
                        v.wait_ge(sSP.h, bSP2 + 48)
                    sDV.inc(v.tensor_copy(rsb[:], rstd_t[:, 0:8]))
                    sDV.wait(v)
                    v.tensor_tensor(
                        xsg[:].rearrange("p (g q) -> p g q", q=1024),
                        gp[:].rearrange("p (g q) -> p g q", q=1024),
                        rsb[:].unsqueeze(2).to_broadcast([128, 8, 1024]),
                        ALU.mult)
                    i = v.tensor_tensor(xsg[:], xsg[:], nw_bc[:], ALU.mult)
                    sDV.inc(i)

                @block.sync
                def _(sy):
                    sDV.wait(sy)
                    sSP.dinc(sy.dma_start(
                        out=sp_gn[t0 - W:t0 - W + 128, :], in_=xsg[:]))
                    if dbg43:
                        r0 = (cc - NWARM) * 128
                        sSP.dinc(sy.dma_start(
                            out=sp_ms[r0:r0 + 128, :], in_=rstd_t[:]))
                        sSP.dinc(sy.dma_start(
                            out=sp_rsb[r0:r0 + 128, :], in_=rsb[:]))
                        sSP.dinc(sy.dma_start(
                            out=sp_xw[r0:r0 + 128, :], in_=xw[:]))
                        sSP.dinc(sy.dma_start(
                            out=sp_gp[r0:r0 + 128, :], in_=gp[:]))

                barrier()

        @block.sync
        def _(sy):
            sSP.dinc(sy.dma_start(out=sp_s, in_=s_st[:]))

        barrier()


def build_rest(nc, block, env):
    g_ = env
    sLD, sGD, sXL, sPE, sDV, sSC, sSP, sST = (
        g_["sLD"], g_["sGD"], g_["sXL"], g_["sPE"], g_["sDV"], g_["sSC"],
        g_["sSP"], g_["sST"])
    sAT = g_["sAT"]
    sW, sSTM, sGP = g_["sW"], g_["sSTM"], g_["sGP"]
    barrier = g_["barrier"]
    sp_gn, sp_attn, sp_xn = g_["sp_gn"], g_["sp_attn"], g_["sp_xn"]
    fl_r, fl_g = g_["fl_r"], g_["fl_g"]
    w_outT, w_up, w_down = g_["w_outT"], g_["w_up"], g_["w_down"]
    w_fc1, w_exp, w_fc2, w_gate = (g_["w_fc1"], g_["w_exp"], g_["w_fc2"],
                                   g_["w_gate"])
    xin, out = g_["xin"], g_["out"]
    gam_t, bet_t = g_["gam_t"], g_["bet_t"]
    onesb, onescb = g_["onesb"], g_["onescb"]
    dbg_stage = g_["dbg_stage"]

    def pbc(ap2d, parts, n):
        return bass.AP(ap2d.tensor, ap2d.offset, [[0, parts], [1, n]])

    with ExitStack() as ctx:
        e = ctx.enter_context
        gn_fm = e(nc.sbuf_tensor("gn_fm", [128, 64 * TMAIN], BF16))
        wob = e(nc.sbuf_tensor("wob", [128, 2 * 64 * 128], BF16))
        att_e = e(nc.sbuf_tensor("att_e", [128, 2 * TMAIN], F32))
        xr = e(nc.sbuf_tensor("xr", [128, 2 * TMAIN], F32))
        ps_o = e(nc.psum_tensor("ps_o", [128, 2 * TMAIN], F32))

        ld0, pe0, dv0, sp0 = sLD.n, sPE.n, sDV.n, sSP.n
        wb0 = [s.n for s in sW]

        @block.sync
        def _(sy):
            for ct in range(64):
                sLD.dinc(sy.dma_start(
                    out=gn_fm[:, ct * TMAIN:(ct + 1) * TMAIN],
                    in_=sp_gn[:, ct * 128:(ct + 1) * 128], transpose=True))
            for db in range(KCH):
                if db >= 2:
                    sy.wait_ge(sDV.h, dv0 + 2 * (db - 1))
                sLD.dinc(sy.dma_start(
                    out=xr[:, (db % 2) * TMAIN:(db % 2) * TMAIN + TMAIN],
                    in_=xin[db * 128:(db + 1) * 128,
                            HALO + W:HALO + W + TMAIN]))

        @block.gpsimd
        def _(g):
            for db in range(KCH):
                if db >= 2:
                    g.wait_ge(sPE.h, pe0 + 2 * (db - 1))
                src = w_outT[:, db * 128:(db + 1) * 128].rearrange(
                    "(kc kp) o -> kp kc o", kp=128)
                dst = wob[:, (db % 2) * 8192:((db % 2) + 1) * 8192].rearrange(
                    "p (kc o) -> p kc o", o=128)
                sW[db % 2].dinc(g.dma_start(out=dst, in_=src))

        @block.tensor
        def _(t):
            t.wait_ge(sLD.h, ld0 + 16 * 64)
            for db in range(KCH):
                t.wait_ge(sW[db % 2].h, wb0[db % 2] + 16 * (db // 2 + 1))
                if db >= 2:
                    t.wait_ge(sDV.h, dv0 + 2 * (db - 1))
                for tb in range(2):
                    for k in range(64):
                        i = t.matmul(
                            ps_o[:, (db % 2) * TMAIN + tb * 512:
                                 (db % 2) * TMAIN + tb * 512 + 512],
                            wob[:, (db % 2) * 8192 + k * 128:
                                (db % 2) * 8192 + (k + 1) * 128],
                            gn_fm[:, k * TMAIN + tb * 512:
                                  k * TMAIN + tb * 512 + 512],
                            start=(k == 0), stop=(k == 63))
                    sPE.inc(i)

        @block.vector
        def _(v):
            for db in range(KCH):
                v.wait_ge(sLD.h, ld0 + 16 * (64 + db + 1))
                if db >= 2:
                    v.wait_ge(sSP.h, sp0 + 16 * (db - 1))
                for tb in range(2):
                    v.wait_ge(sPE.h, pe0 + 2 * db + tb + 1)
                    sl = (db % 2) * TMAIN + tb * 512
                    sDV.inc(v.tensor_tensor(att_e[:, sl:sl + 512],
                                            ps_o[:, sl:sl + 512],
                                            xr[:, sl:sl + 512], ALU.add))

        @block.scalar
        def _(s):
            for db in range(KCH):
                s.wait_ge(sDV.h, dv0 + 2 * db + 2)
                sSP.dinc(s.dma_start(
                    out=sp_attn[db * 128:(db + 1) * 128, :],
                    in_=att_e[:, (db % 2) * TMAIN:(db % 2) * TMAIN + TMAIN]))

        barrier()

    if dbg_stage == 4:
        return

    NLB, NFB, NEB, NHB = 16, 64, 16, 32
    NGT = NLB + 1 + NFB + NEB + NHB
    WSLOT = 64 * 128
    kinds = []
    for th in (0, 1):
        kinds += [("fc1", th, i) for i in range(NLB)]
        kinds += [("gate", th, 0)]
        kinds += [("up", th, i) for i in range(NFB)]
        kinds += [("exp", th, i) for i in range(NEB)]
        kinds += [("dwn", th, i) for i in range(NHB)]
    NG = len(kinds)
    wneed, wcum = [], [0, 0]
    for gi, (kd, th, ix) in enumerate(kinds):
        wcum[gi % 2] += 2 if kd == "dwn" else 1
        wneed.append(wcum[gi % 2])
    GUP7 = [th * NGT + NLB + 1 + 7 for th in (0, 1)]
    GUPL = [th * NGT + NLB + 1 + NFB - 1 for th in (0, 1)]
    GDW0 = [th * NGT + NLB + 1 + NFB + NEB for th in (0, 1)]

    with ExitStack() as ctx:
        e = ctx.enter_context
        xnb = e(nc.sbuf_tensor("xnb", [128, 32 * 512], BF16))
        hb = e(nc.sbuf_tensor("hb", [128, 64 * 512], BF16))
        xlat = e(nc.sbuf_tensor("xlat", [128, 16 * 512], BF16))
        eo = e(nc.sbuf_tensor("eo", [128, 16 * 512], BF16))
        pre = e(nc.sbuf_tensor("pre", [128, 32 * 512], BF16))
        wbuf = e(nc.sbuf_tensor("wbuf", [128, 2 * WSLOT], BF16))
        sqb = e(nc.sbuf_tensor("sqb", [128, 2 * 512], BF16))
        gl = sqb[0:8, 0:512]
        rowb2 = sqb[0:1, 0:1024]
        lg_t = e(nc.sbuf_tensor("lg_t", [128, 4 * 16], BF16))
        mx = e(nc.sbuf_tensor("mx", [128, 4 * 8], F32))
        r_t = e(nc.sbuf_tensor("r_t", [128, 4], F32))
        rbc = e(nc.sbuf_tensor("rbc", [128, 512], BF16))
        rowf = e(nc.sbuf_tensor("rowf2", [1, 2 * 512], F32))
        att_t = e(nc.sbuf_tensor("att_t", [128, 512], F32))
        ofb = e(nc.sbuf_tensor("ofb", [128, 512], F32))
        ps_m = e(nc.psum_tensor("ps_m", [128, 2 * 512], F32))
        ps_g = e(nc.psum_tensor("ps_g", [8, 512], F32))
        ps_s1 = e(nc.psum_tensor("ps_s1", [1, 512], F32))
        ps_s2 = e(nc.psum_tensor("ps_s2", [1, 512], F32))
        ps_bc = e(nc.psum_tensor("ps_bc", [128, 2 * 512], F32))

        ld0, gd0, pe0, dv0 = sLD.n, sGD.n, sPE.n, sDV.n
        sc0, sp0, st0, stm0, gp0, xl0 = (sSC.n, sSP.n, sST.n, sSTM.n,
                                         sGP.n, sXL.n)
        wb0 = [s.n for s in sW]

        xnb_cnt = [ld0 + 16 * 32, ld0 + 16 * (32 + 4 + 32)]
        tr_cnt = [ld0 + 16 * 36, ld0 + 16 * 72]
        gl_cnt = [sp0 + 16, sp0 + 96]
        flr_cnt = [sp0 + 80, sp0 + 160]
        def dv_evac(gi):
            return dv0 + gi + 1
        gp = gp0
        rt_done, xlr_done, rows_read, rows_done = {}, {}, {}, {}
        jdone = {}
        for th in (0, 1):
            if th == 1:
                for hb_i in range(NHB):
                    gp += 4
                    jdone[(0, hb_i)] = gp
            gp += 8
            rt_done[th] = gp
            gp += 16
            xlr_done[th] = gp
            gp += 2
            rows_read[th] = gp
            gp += 2
            rows_done[th] = gp
        for hb_i in range(NHB):
            gp += 4
            jdone[(1, hb_i)] = gp
        gp = gp0
        rt_done, xlr_done, rows_read, rows_done, jdone = {}, {}, {}, {}, {}
        gp += 8; rt_done[0] = gp
        gp += 16; xlr_done[0] = gp
        gp += 2; rows_read[0] = gp
        gp += 2; rows_done[0] = gp
        for hb_i in range(NHB):
            gp += 4
            jdone[(0, hb_i)] = gp
        gp += 8; rt_done[1] = gp
        gp += 16; xlr_done[1] = gp
        gp += 2; rows_read[1] = gp
        gp += 2; rows_done[1] = gp
        for hb_i in range(NHB):
            gp += 4
            jdone[(1, hb_i)] = gp
        gp_final = gp
        sq_cnt, rowb_cnt = {}, {}
        sc = sc0
        for th in (0, 1):
            for hb_i in range(NHB):
                sc += 1
                sq_cnt[(th, hb_i)] = sc
            sc += 2
            sc += 2
            rowb_cnt[th] = sc
        sc_final = sc
        stats_done = {0: stm0 + NHB, 1: stm0 + 2 * NHB + 1 + 1 - 1}
        stats_done[1] = stm0 + NHB + 1 + NHB
        bcast_done = {0: stm0 + NHB + 1, 1: stm0 + 2 * NHB + 2}
        at0 = sAT.n
        att_cnt = {(th, hb_i): at0 + 16 * (th * NHB + hb_i + 1)
                   for th in (0, 1) for hb_i in range(NHB)}
        st_cnt = {(th, hb_i): st0 + 16 * (th * NHB + hb_i + 1)
                  for th in (0, 1) for hb_i in range(NHB)}
        st_final = st0 + 16 * 64

        @block.sync
        def _(sy):
            for th in (0, 1):
                if th == 1:
                    sy.wait_ge(sPE.h, pe0 + GUPL[0] + 1)
                for k in range(KCH):
                    sLD.dinc(sy.dma_start(
                        out=xnb[:, k * 512:(k + 1) * 512],
                        in_=sp_xn[k * 128:(k + 1) * 128,
                                  HALO + W + th * 512:
                                  HALO + W + th * 512 + 512]))
                sy.wait_ge(sDV.h, dv_evac(th * NGT + NLB))
                sSP.dinc(sy.dma_start(out=fl_g[0:8, :], in_=gl))
                sy.wait_ge(sSP.h, gl_cnt[th])
                for c in range(4):
                    sLD.dinc(sy.dma_start(
                        out=lg_t[:, c * 16:(c + 1) * 16],
                        in_=fl_g[:, c * 128:(c + 1) * 128], transpose=True))
                sy.wait_ge(sGP.h, rt_done[th])
                for c in range(4):
                    sSP.dinc(sy.dma_start(
                        out=fl_r[th:th + 1, c * 128:(c + 1) * 128]
                        .rearrange("o t -> (o t)").unsqueeze(1),
                        in_=r_t[:, c:c + 1]))

        @block.gpsimd
        def _(g):
            for gi, (kd, th, ix) in enumerate(kinds):
                sl = gi % 2
                if gi >= 2:
                    g.wait_ge(sPE.h, pe0 + gi - 1)
                if kd == "fc1":
                    srcs = [(w_fc1[:, ix * 128:(ix + 1) * 128], 32, 128)]
                elif kd == "gate":
                    srcs = [(w_gate, 32, 8)]
                elif kd == "up":
                    srcs = [(w_up[:, ix * 128:(ix + 1) * 128], 32, 128)]
                elif kd == "exp":
                    srcs = [(w_exp[:, ix * 128:(ix + 1) * 128], 16, 128)]
                else:
                    srcs = [(w_down[:, ix * 128:(ix + 1) * 128], 64, 128),
                            (w_fc2[:, ix * 128:(ix + 1) * 128], 16, 128)]
                if kd == "dwn" and ix < 2:
                    g.wait_ge(sPE.h, pe0 + GDW0[th])
                off = sl * WSLOT
                for si, (ap, kc, oo) in enumerate(srcs):
                    if kd == "dwn" and si == 1:
                        dst = xlat[:, sl * 2048:sl * 2048 + 2048].rearrange(
                            "p (kc o) -> p kc o", o=128)
                    else:
                        dst = wbuf[:, off:off + kc * oo].rearrange(
                            "p (kc o) -> p kc o", o=oo)
                    sW[sl].dinc(g.dma_start(
                        out=dst, in_=ap.rearrange("(kc kp) o -> kp kc o",
                                                  kp=128)))
                    off += kc * oo
                if kd == "up" and ix == NFB - 1:
                    if th == 1:
                        g.wait_ge(sGP.h, xlr_done[0])
                    g.wait_ge(sSP.h, flr_cnt[th])
                    sGD.dinc(g.dma_start(
                        out=rbc[:], in_=pbc(fl_r[th:th + 1, :], 128, 512)))
        rbc_cnt = {0: gd0 + 16, 1: gd0 + 32}

        @block.tensor
        def _(t):
            def emit_stats(th, hb_i):
                t.wait_ge(sSC.h, sq_cnt[(th, hb_i)])
                if th == 1 and hb_i == 0:
                    t.wait_ge(sSC.h, rowb_cnt[0])
                t.matmul(ps_s1[:], onescb[:],
                         pre[:, hb_i * 512:(hb_i + 1) * 512],
                         start=(hb_i == 0), stop=(hb_i == NHB - 1),
                         skip_group_check=True)
                i = t.matmul(ps_s2[:], onescb[:],
                             sqb[:, (hb_i % 2) * 512:(hb_i % 2) * 512 + 512],
                             start=(hb_i == 0), stop=(hb_i == NHB - 1),
                             skip_group_check=True)
                sSTM.inc(i)

            def emit_bcast(th):
                t.wait_ge(sSC.h, rowb_cnt[th])
                if th == 1:
                    t.wait_ge(sGP.h, jdone[(0, NHB - 1)])
                t.matmul(ps_bc[:, 0:512], onesb[:], rowb2[0:1, 0:512],
                         start=True, stop=True)
                i = t.matmul(ps_bc[:, 512:1024], onesb[:],
                             rowb2[0:1, 512:1024], start=True, stop=True)
                sSTM.inc(i)

            for gi, (kd, th, ix) in enumerate(kinds):
                sl = gi % 2
                t.wait_ge(sW[sl].h, wb0[sl] + 16 * wneed[gi])
                if gi >= 2:
                    t.wait_ge(sDV.h, dv_evac(gi - 2))
                if kd in ("fc1", "gate", "up"):
                    t.wait_ge(sLD.h, xnb_cnt[th])
                if kd == "exp":
                    t.wait_ge(sGP.h, xlr_done[th])
                if kd == "dwn":
                    t.wait_ge(sDV.h, dv_evac(GDW0[th] - 1))
                off = sl * WSLOT
                ps = ps_m[:, sl * 512:(sl + 1) * 512]
                if kd == "fc1" or kd == "up":
                    for k in range(KCH):
                        i = t.matmul(ps, wbuf[:, off + k * 128:
                                              off + (k + 1) * 128],
                                     xnb[:, k * 512:(k + 1) * 512],
                                     start=(k == 0), stop=(k == KCH - 1))
                elif kd == "gate":
                    for k in range(KCH):
                        i = t.matmul(ps_g[:], wbuf[:, off + k * 8:
                                                   off + (k + 1) * 8],
                                     xnb[:, k * 512:(k + 1) * 512],
                                     start=(k == 0), stop=(k == KCH - 1))
                elif kd == "exp":
                    for k in range(NEB):
                        i = t.matmul(ps, wbuf[:, off + k * 128:
                                              off + (k + 1) * 128],
                                     xlat[:, k * 512:(k + 1) * 512],
                                     start=(k == 0), stop=(k == NEB - 1))
                else:
                    for k in range(NFB):
                        t.matmul(ps, wbuf[:, off + k * 128:
                                          off + (k + 1) * 128],
                                 hb[:, k * 512:(k + 1) * 512],
                                 start=(k == 0), stop=False,
                                 skip_group_check=True)
                    for k in range(NEB):
                        i = t.matmul(ps, xlat[:, sl * 2048 + k * 128:
                                              sl * 2048 + (k + 1) * 128],
                                     eo[:, k * 512:(k + 1) * 512],
                                     start=False, stop=(k == NEB - 1),
                                     skip_group_check=True)
                sPE.inc(i)
                if kd == "dwn":
                    if ix >= 1:
                        emit_stats(th, ix - 1)
                    if ix == NHB - 1:
                        emit_stats(th, NHB - 1)
                        emit_bcast(th)

        @block.vector
        def _(v):
            jq = []

            def gp_hop(i):
                sGP.inc(i)
                v.wait_ge(sGP.h, sGP.n)

            def emit_rchain(th):
                v.wait_ge(sLD.h, tr_cnt[th])
                for c in range(4):
                    sGP.inc(v.max(mx[:, c * 8:(c + 1) * 8],
                                  lg_t[:, c * 16:c * 16 + 8]))
                v.wait_ge(sGP.h, sGP.n)
                for c in range(4):
                    sGP.inc(v.tensor_tensor(r_t[:, c:c + 1],
                                            mx[:, c * 8:c * 8 + 1],
                                            mx[:, c * 8 + 1:c * 8 + 2],
                                            ALU.add))
                v.wait_ge(sGP.h, sGP.n)
                assert sGP.n == rt_done[th]

            def emit_xlatr(th):
                v.wait_ge(sGD.h, rbc_cnt[th])
                for k in range(NEB):
                    sGP.inc(v.tensor_tensor(
                        xlat[:, k * 512:(k + 1) * 512],
                        xlat[:, k * 512:(k + 1) * 512], rbc[:], ALU.mult))
                assert sGP.n == xlr_done[th]

            def emit_rows(th):
                v.wait_ge(sSTM.h, stats_done[th])
                mu_r = rowf[:, 0:512]
                m2 = rowf[:, 512:1024]
                t2 = sqb[0:1, 0:512]
                sGP.inc(v.tensor_scalar(mu_r, ps_s1[:], 1.0 / HID, None,
                                        ALU.mult))
                gp_hop(v.tensor_scalar(m2, ps_s2[:], 1.0 / HID, None,
                                       ALU.mult))
                assert sGP.n == rows_read[th]
                gp_hop(v.tensor_tensor(t2, mu_r, mu_r, ALU.mult))
                gp_hop(v.tensor_tensor(m2, m2, t2, ALU.subtract))
                assert sGP.n == rows_done[th]

            def emit_j(th, hb_i):
                v.wait_ge(sSTM.h, bcast_done[th])
                v.wait_ge(sAT.h, att_cnt[(th, hb_i)])
                if hb_i >= 1 or th == 1:
                    pv = (th, hb_i - 1) if hb_i >= 1 else (0, NHB - 1)
                    v.wait_ge(sST.h, st_cnt[pv])
                psl = pre[:, hb_i * 512:(hb_i + 1) * 512]
                sGP.inc(v.tensor_tensor(ofb[:], psl, ps_bc[:, 0:512],
                                        ALU.subtract))
                sGP.inc(v.tensor_tensor(ofb[:], ofb[:], ps_bc[:, 512:1024],
                                        ALU.mult))
                sGP.inc(v.tensor_scalar(ofb[:], ofb[:],
                                        gam_t[:, hb_i:hb_i + 1],
                                        bet_t[:, hb_i:hb_i + 1],
                                        ALU.mult, ALU.add))
                sGP.inc(v.tensor_tensor(ofb[:], ofb[:], att_t[:], ALU.add))
                assert sGP.n == jdone[(th, hb_i)]

            for gi, (kd, th, ix) in enumerate(kinds):
                v.wait_ge(sPE.h, pe0 + gi + 1)
                if kd == "fc1":
                    i = v.tensor_copy(xlat[:, ix * 512:(ix + 1) * 512],
                                      ps_m[:, (gi % 2) * 512:
                                           (gi % 2) * 512 + 512])
                elif kd == "gate":
                    i = v.tensor_copy(gl, ps_g[:])
                elif kd == "up":
                    psl = ps_m[:, (gi % 2) * 512:(gi % 2) * 512 + 512]
                    v.tensor_scalar(sqb[:, 512:1024], psl, 0.0, None,
                                    ALU.max)
                    i = v.tensor_tensor(hb[:, ix * 512:(ix + 1) * 512],
                                        sqb[:, 512:1024], psl, ALU.mult)
                elif kd == "exp":
                    i = v.tensor_copy(eo[:, ix * 512:(ix + 1) * 512],
                                      ps_m[:, (gi % 2) * 512:
                                           (gi % 2) * 512 + 512])
                else:
                    i = v.tensor_copy(pre[:, ix * 512:(ix + 1) * 512],
                                      ps_m[:, (gi % 2) * 512:
                                           (gi % 2) * 512 + 512])
                sDV.inc(i)
                assert sDV.n == dv_evac(gi)
                if th == 1 and jq:
                    emit_j(*jq.pop(0))
                if kd == "up" and ix == 7:
                    while jq:
                        emit_j(*jq.pop(0))
                    emit_rchain(th)
                if kd == "up" and ix == NFB - 1:
                    emit_xlatr(th)
                if kd == "dwn" and ix == NHB - 1:
                    emit_rows(th)
                    if th == 0:
                        jq += [(0, h2) for h2 in range(NHB)]
            while jq:
                emit_j(*jq.pop(0))
            for h2 in range(NHB):
                emit_j(1, h2)
            assert sGP.n == gp_final

        @block.scalar
        def _(s):
            for th in (0, 1):
                for hb_i in range(NHB):
                    s.wait_ge(sDV.h, dv_evac(GDW0[th] + hb_i))
                    if hb_i >= 2:
                        s.wait_ge(sSTM.h, stm0 + th * (NHB + 1) + hb_i - 1)
                    i = s.activation(sqb[:, (hb_i % 2) * 512:
                                         (hb_i % 2) * 512 + 512],
                                     pre[:, hb_i * 512:(hb_i + 1) * 512],
                                     AF.Square, bias=0.0, scale=1.0)
                    sSC.inc(i)
                    assert sSC.n == sq_cnt[(th, hb_i)]
                s.wait_ge(sGP.h, rows_done[th])
                m2 = rowf[:, 512:1024]
                sSC.inc(s.activation(m2, m2, AF.Ln, bias=EPS, scale=1.0))
                s.wait_ge(sSC.h, sSC.n)
                sSC.inc(s.activation(m2, m2, AF.Exp, bias=0.0, scale=-0.5))
                s.wait_ge(sSC.h, sSC.n)
                s.copy(rowb2[0:1, 0:512], rowf[:, 0:512])
                sSC.inc(s.copy(rowb2[0:1, 512:1024], m2))
                sSC.inc(s.nop())
                assert sSC.n == rowb_cnt[th]
                for hb_i in range(NHB):
                    if hb_i >= 1:
                        s.wait_ge(sGP.h, jdone[(th, hb_i - 1)])
                    sAT.dinc(s.dma_start(
                        out=att_t[:],
                        in_=sp_attn[hb_i * 128:(hb_i + 1) * 128,
                                    th * 512:th * 512 + 512]))
                    s.wait_ge(sGP.h, jdone[(th, hb_i)])
                    sST.dinc(s.dma_start(
                        out=out[hb_i * 128:(hb_i + 1) * 128,
                                th * 512:th * 512 + 512],
                        in_=ofb[:]))

        @block.gpsimd
        def _(g):
            g.wait_ge(sST.h, st_final)
            g.wait_ge(sAT.h, at0 + 16 * 64)
        barrier()


def _prep_inputs(kw):
    x = np.ascontiguousarray(kw["L0_attn_input_states"])
    shared = {
        "w_projT": np.ascontiguousarray(kw["L0_attn_in_proj_w"].T),
        "w_outT": np.ascontiguousarray(kw["L0_attn_out_w"].T),
        "w_up": np.ascontiguousarray(kw["L0_ffn_up_w"]),
        "w_down": np.ascontiguousarray(kw["L0_ffn_down_w"]),
        "w_fc1": np.ascontiguousarray(kw["L0_ffn_fc1_w"]),
        "w_exp": np.ascontiguousarray(kw["L0_ffn_expert_w"]),
        "w_fc2": np.ascontiguousarray(kw["L0_ffn_fc2_w"]),
        "w_gate": np.ascontiguousarray(kw["L0_ffn_gate_w"]),
        "conv_wT": np.ascontiguousarray(
            np.asarray(kw["L0_attn_conv_w"]).reshape(CONV // 128, 128, 4)
            .transpose(1, 0, 2).reshape(128, (CONV // 128) * 4)),
        "conv_bT": np.ascontiguousarray(
            np.asarray(kw["L0_attn_conv_b"]).reshape(CONV // 128, 128).T),
        "dt_bias": np.ascontiguousarray(np.asarray(kw["L0_attn_dt_bias"]).reshape(128, 1)),
        "a_log": np.ascontiguousarray(np.asarray(kw["L0_attn_A_log"]).reshape(128, 1)),
        "d_exp": np.ascontiguousarray(
            np.repeat(np.asarray(kw["L0_attn_D"]), P).reshape(1, INTER)),
        "nw_exp": np.ascontiguousarray(
            np.asarray(kw["L0_attn_norm_w"]).reshape(1, INTER)),
        "normw_T": np.ascontiguousarray(
            np.asarray(kw["L0_attn_norm_w"]).reshape(INTER // 128, 128).T),
        "gam_T": np.ascontiguousarray(
            np.asarray(kw["L0_ffn_ln_gamma"]).reshape(HID // 128, 128).T),
        "bet_T": np.ascontiguousarray(
            np.asarray(kw["L0_ffn_ln_beta"]).reshape(HID // 128, 128).T),
        "m0T": np.ascontiguousarray(
            np.where(np.arange(128)[:, None] >= np.arange(128)[None, :],
                     0.0, NEG).astype(ml_dtypes.bfloat16)),
        "idb": np.ascontiguousarray(np.eye(128, dtype=ml_dtypes.bfloat16)),
        "idf": np.ascontiguousarray(np.eye(128, dtype=np.float32)),
    }
    shared = {k: np.ascontiguousarray(np.asarray(v)) for k, v in shared.items()}
    ins = []
    for k in range(NCORES):
        b, q = k // 4, k % 4
        lo = q * TMAIN - (HALO + W)
        if lo >= 0:
            xs = x[b, lo:(q + 1) * TMAIN, :].T
        else:
            pad = np.zeros((HID, -lo), np.float32)
            xs = np.concatenate([pad, x[b, 0:(q + 1) * TMAIN, :].T], axis=1)
        m = dict(shared)
        m["xin"] = np.ascontiguousarray(xs.astype(np.float32))
        m["warm_mask"] = np.full((128, 1), 0.0 if q == 0 else 1.0, np.float32)
        ins.append(m)
    return ins


def kernel(**kw):
    ins = _prep_inputs(kw)
    nc = build()
    res = run_bass_kernel_spmd(nc, ins, core_ids=list(range(NCORES)))
    outf = np.empty((B, L, HID), np.float32)
    for k in range(NCORES):
        b, q = k // 4, k % 4
        outf[b, q * TMAIN:(q + 1) * TMAIN, :] = res.results[k]["out"].T
    return outf
